# version 09
# baseline (speedup 1.0000x reference)
import numpy as np
from contextlib import ExitStack

import concourse.bass as bass
import concourse.mybir as mybir
from concourse.bass_utils import run_bass_kernel_spmd

F32 = mybir.dt.float32
BF16 = mybir.dt.bfloat16
AF = mybir.ActivationFunctionType
ALU = mybir.AluOpType

NCORES = 8
DM = 1024
DI = 2048
NST = 16
RK = 64
HALO = 99
TR = 4096
T = TR + HALO
E_EPS = float(np.exp(np.float64(1e-4)))
LNEPS = 1e-5
RMSEPS = 1e-6

import os
A_CHUNKS = [(0, 1024), (1024, 1024), (2048, 1024), (3072, 1024), (4096, HALO)]
if os.environ.get("DEV_ACHUNKS"):
    A_CHUNKS = A_CHUNKS[: int(os.environ["DEV_ACHUNKS"])]
NCH = len(A_CHUNKS)
NPS_GLOBAL = 16
if os.environ.get("DEV_BPAIRS"):
    NPS_GLOBAL = 2 * int(os.environ["DEV_BPAIRS"])
B_SEGS = [(0, 2147, 99, 2048, 0), (2147, 2048, 0, 2048, 2048)]
LC = 512
NCC = TR // LC


def _halves(L):
    return [(h, min(512, L - h)) for h in range(0, L, 512)]


def build_nc(upto="C"):
    nc = bass.Bass("TRN2", target_bir_lowering=False,
                   detect_race_conditions=False)

    def din(name, shape, dtype=F32):
        return nc.dram_tensor(name, shape, dtype, kind="ExternalInput").ap()

    xT_in = din("xT_in", [DM, T])
    w_inT = din("w_inT", [DM, 2 * DI])
    w_xT = din("w_xT", [DI, RK + 2 * NST])
    w_dtT = din("w_dtT", [RK, DI])
    w_outT = din("w_outT", [DI, DM])
    w_f1 = din("w_f1", [DM, DM])
    w_f2 = din("w_f2", [DM, DM])
    w_f3 = din("w_f3", [DM, DM])
    a_re = din("a_re", [128, NST * NST])
    wc_re = din("wc_re", [128, 16 * 4])
    bconv_re = din("bconv_re", [128, 16])
    bdt_re = din("bdt_re", [128, 16])
    d_re = din("d_re", [128, 16])
    rmsw_re = din("rmsw_re", [128, 16])
    fbias_re = din("fbias_re", [128, 8])
    bmask_in = din("bmask_in", [1, HALO])
    consts_in = din("consts_in", [128, 8])
    ident_in = din("ident_in", [128, 128], BF16)

    def dscr(name, shape, out=False):
        kind = "ExternalOutput" if out else "Internal"
        return nc.dram_tensor(name, shape, F32, kind=kind).ap()

    xc_d = dscr("xc_d", [DI, T], out=(upto == "A"))
    siluz_d = dscr("siluz_d", [DI, T], out=(upto == "A"))
    dtsp_d = dscr("dtsp_d", [DI, T], out=(upto == "A"))
    bc_d = dscr("bc_d", [2 * NST, T], out=(upto == "A"))
    yg_d = dscr("yg_d", [DI, TR], out=(upto == "B"))
    out_d = dscr("out_d", [DM, TR], out=(upto == "C"))

    with ExitStack() as outer:
        E = outer.enter_context
        absA = E(nc.sbuf_tensor("absA", [128, NST * NST], F32))
        thr = E(nc.sbuf_tensor("thr", [128, NST * NST], F32))
        wc_sb = E(nc.sbuf_tensor("wc_sb", [128, 64], F32))
        bconv_sb = E(nc.sbuf_tensor("bconv_sb", [128, 16], F32))
        bdt_sb = E(nc.sbuf_tensor("bdt_sb", [128, 16], F32))
        dvec_sb = E(nc.sbuf_tensor("dvec_sb", [128, 16], F32))
        rmsw_sb = E(nc.sbuf_tensor("rmsw_sb", [128, 16], F32))
        fbias_sb = E(nc.sbuf_tensor("fbias_sb", [128, 8], F32))
        bmask_bc = E(nc.sbuf_tensor("bmask_bc", [128, HALO], F32))
        consts_sb = E(nc.sbuf_tensor("consts_sb", [128, 8], F32))
        C_EEPS = consts_sb[:, 0:1]
        C_N10 = consts_sb[:, 1:2]
        C_M1 = consts_sb[:, 2:3]
        C_IDI = consts_sb[:, 3:4]
        C_RMSE = consts_sb[:, 4:5]
        C_MH = consts_sb[:, 5:6]
        C_LNE = consts_sb[:, 6:7]
        C1_IDI = consts_sb[0:1, 3:4]
        C1_RMSE = consts_sb[0:1, 4:5]
        C1_MH = consts_sb[0:1, 5:6]
        C1_LNE = consts_sb[0:1, 6:7]
        ident_sb = E(nc.sbuf_tensor("ident_sb", [128, 128], BF16))
        ones_col = E(nc.sbuf_tensor("ones_col", [128, 1], F32))
        ones_row = E(nc.sbuf_tensor("ones_row", [1, 128], F32))
        state_sb = E(nc.sbuf_tensor("state_sb", [128, 32], F32))

        S = {}
        for s_ in ["c", "gw", "gx", "mma", "eva", "sxin", "vconv", "sxc", "gxc",
                   "oxc", "oz", "sz", "mmx", "evx", "sdtr", "mmdt", "evdt",
                   "sdt", "odtsp", "obc",
                   "slab", "gslab", "xcl", "sda", "vscan", "vhc", "mmb", "gate", "oyg", "sstate",
                   "gwc", "iyg", "ixr", "sqr", "msqr", "sql1", "msql1",
                   "sql2", "msql2", "srstd_r", "srstd_1", "srstd_2", "mbcr", "mbc1", "mbc2",
                   "vyn", "matt", "vr1", "mstat1", "mstat2", "vstat", "vln", "gln",
                   "mf", "sf", "vr2", "vout", "oout"]:
            S[s_] = E(nc.semaphore(s_))

        with nc.Block() as blk0:
            @blk0.sync
            def _(sy):
                for t_, src in [(absA, a_re), (wc_sb, wc_re), (bconv_sb, bconv_re),
                                (bdt_sb, bdt_re), (dvec_sb, d_re), (rmsw_sb, rmsw_re),
                                (fbias_sb, fbias_re), (ident_sb, ident_in),
                                (consts_sb, consts_in)]:
                    sy.dma_start(out=t_[:], in_=src).then_inc(S["c"], 16)
                src = bass.AP(bmask_in.tensor, 0, [[0, 128], [1, HALO]])
                sy.dma_start(out=bmask_bc[:], in_=src).then_inc(S["c"], 16)

            @blk0.scalar
            def _(s):
                s.wait_ge(S["c"], 16 * 10)
                s.activation(out=absA[:], in_=absA[:], func=AF.Exp).then_inc(S["c"], 1)

        with nc.Block() as blk0b:
            @blk0b.vector
            def _(v):
                v.memset(ones_col[:], 1.0)
                v.memset(ones_row[:], 1.0)
                v.wait_ge(S["c"], 16 * 10 + 1)
                v.reciprocal(thr[:], absA[:])
                v.tensor_scalar_mul(thr[:], thr[:], 10.0).then_inc(S["c"], 1)

        with ExitStack() as pa:
            EA = pa.enter_context
            w_inT_bf = EA(nc.sbuf_tensor("w_inT_bf", [128, 8, 2 * DI], BF16))
            w_xT_bf = EA(nc.sbuf_tensor("w_xT_bf", [128, 16, RK + 2 * NST], BF16))
            w_dtT_bf = EA(nc.sbuf_tensor("w_dtT_bf", [RK, DI], BF16))
            xT_bf = [EA(nc.sbuf_tensor(f"xT_bf{i}", [128, 8, 1024], BF16)) for i in range(2)]
            xin_t = [EA(nc.sbuf_tensor(f"xin_t{i}", [128, 1027], F32)) for i in range(2)]
            xcp_t = [EA(nc.sbuf_tensor(f"xcp_t{i}", [128, 1024], F32)) for i in range(2)]
            xc_t = [EA(nc.sbuf_tensor(f"xc_t{i}", [128, 1024], F32)) for i in range(2)]
            z_t = [EA(nc.sbuf_tensor(f"z_t{i}", [128, 1024], F32)) for i in range(2)]
            xcbf = EA(nc.sbuf_tensor("xcbf", [128, 16, 1024], BF16))
            dtr_bf = EA(nc.sbuf_tensor("dtr_bf", [RK, 1024], BF16))
            dte_t = EA(nc.sbuf_tensor("dte_t", [128, 1024], F32))
            dtsp_t = [EA(nc.sbuf_tensor(f"dtsp_t{i}", [128, 1024], F32)) for i in range(2)]
            bch_t = [EA(nc.sbuf_tensor(f"bch_t{i}", [96, 1024], F32)) for i in range(2)]
            tails = EA(nc.sbuf_tensor("tails", [128, 48], F32))
            xz_ps = [EA(nc.psum_tensor(f"xz_ps{i}", [128, 1024], F32)) for i in range(2)]
            dbl_ps = EA(nc.psum_tensor("dbl_ps", [RK + 2 * NST, 1024], F32))
            dt_ps = EA(nc.psum_tensor("dt_ps", [128, 1024], F32))

            with nc.Block() as blk:
                @blk.gpsimd
                def _(g):
                    src = bass.AP(w_inT.tensor, 0,
                                  [[2 * DI, 128], [128 * 2 * DI, 8], [1, 2 * DI]])
                    g.dma_start(out=w_inT_bf[:], in_=src).then_inc(S["gw"], 16)
                    ww = RK + 2 * NST
                    src = bass.AP(w_xT.tensor, 0, [[ww, 128], [128 * ww, 16], [1, ww]])
                    g.dma_start(out=w_xT_bf[:], in_=src).then_inc(S["gw"], 16)
                    g.dma_start(out=w_dtT_bf[:], in_=w_dtT).then_inc(S["gw"], 16)
                    for j, (c0, L) in enumerate(A_CHUNKS):
                        if j >= 2:
                            g.wait_ge(S["mma"], (j - 1) * 32)
                        src = bass.AP(xT_in.tensor, c0, [[T, 128], [128 * T, 8], [1, L]])
                        g.dma_start(out=xT_bf[j % 2][:, :, 0:L], in_=src
                                    ).then_inc(S["gx"], 16)
                        if j >= 1:
                            g.wait_ge(S["mmx"], j)
                        for e in range(16):
                            g.wait_ge(S["sxc"], j * 16 + e + 1)
                            g.dma_start(out=xcbf[:, e, 0:L], in_=xc_t[e % 2][:, 0:L]
                                        ).then_inc(S["gxc"], 16)

                @blk.tensor
                def _(t):
                    t.wait_ge(S["gw"], 48)
                    for j, (c0, L) in enumerate(A_CHUNKS):
                        t.wait_ge(S["gx"], 16 * (j + 1))
                        for e in range(32):
                            ge = j * 32 + e
                            if ge >= 2:
                                t.wait_ge(S["eva"], ge - 1)
                            for h0, hl in _halves(L):
                                for k in range(8):
                                    inst = t.matmul(
                                        xz_ps[e % 2][:, h0:h0 + hl],
                                        w_inT_bf[:, k, e * 128:(e + 1) * 128],
                                        xT_bf[j % 2][:, k, h0:h0 + hl],
                                        start=(k == 0), stop=(k == 7))
                            inst.then_inc(S["mma"], 1)
                        t.wait_ge(S["gxc"], 256 * (j + 1))
                        if j >= 1:
                            t.wait_ge(S["evx"], 2 * j)
                        for h0, hl in _halves(L):
                            for k in range(16):
                                inst = t.matmul(
                                    dbl_ps[:, h0:h0 + hl], w_xT_bf[:, k, :],
                                    xcbf[:, k, h0:h0 + hl],
                                    start=(k == 0), stop=(k == 15))
                        inst.then_inc(S["mmx"], 1)
                        t.wait_ge(S["sdtr"], j + 1)
                        for di_ in range(16):
                            gd = j * 16 + di_
                            if gd >= 1:
                                t.wait_ge(S["evdt"], gd)
                            for h0, hl in _halves(L):
                                inst = t.matmul(
                                    dt_ps[:, h0:h0 + hl],
                                    w_dtT_bf[:, di_ * 128:(di_ + 1) * 128],
                                    dtr_bf[:, h0:h0 + hl],
                                    start=True, stop=True)
                            inst.then_inc(S["mmdt"], 1)

                @blk.scalar
                def _(s):
                    def emit_silu(j, L, e):
                        s.wait_ge(S["vconv"], j * 16 + e + 1)
                        ge16 = j * 16 + e
                        if ge16 >= 2:
                            s.wait_ge(S["oxc"], 16 * (ge16 - 1))
                            s.wait_ge(S["gxc"], 16 * (ge16 - 1))
                        s.activation(out=xc_t[e % 2][:, 0:L],
                                     in_=xcp_t[e % 2][:, 0:L], func=AF.Silu
                                     ).then_inc(S["sxc"], 1)

                    for j, (c0, L) in enumerate(A_CHUNKS):
                        for e in range(32):
                            s.wait_ge(S["mma"], j * 32 + e + 1)
                            if e < 16:
                                ge16 = j * 16 + e
                                if ge16 >= 2:
                                    s.wait_ge(S["vconv"], ge16 - 1)
                                if j > 0:
                                    s.copy(xin_t[e % 2][:, 0:3], tails[:, 3 * e:3 * e + 3])
                                else:
                                    s.memzero(xin_t[e % 2][:, 0:3])
                                s.copy(xin_t[e % 2][:, 3:3 + L], xz_ps[e % 2][:, 0:L]
                                       ).then_inc(S["eva"], 1)
                                if j < NCH - 1:
                                    s.copy(tails[:, 3 * e:3 * e + 3],
                                           xin_t[e % 2][:, L:L + 3])
                                s.copy(dte_t[:, 0:1], xin_t[e % 2][:, 0:1]
                                       ).then_inc(S["sxin"], 1)
                            else:
                                ez = e - 16
                                gz = j * 16 + ez
                                if gz >= 2:
                                    s.wait_ge(S["oz"], 16 * (gz - 1))
                                s.activation(out=z_t[ez % 2][:, 0:L],
                                             in_=xz_ps[e % 2][:, 0:L], func=AF.Silu
                                             ).then_inc(S["eva"], 1)
                                s.copy(dte_t[:, 0:1], z_t[ez % 2][:, 0:1]
                                       ).then_inc(S["sz"], 1)
                            if 2 <= e < 18:
                                emit_silu(j, L, e - 2)
                        s.wait_ge(S["mmx"], j + 1)
                        s.copy(dtr_bf[:, 0:L], dbl_ps[0:RK, 0:L]).then_inc(S["evx"], 1)
                        s.copy(dte_t[0:1, 0:1], dte_t[0:1, 0:1]).then_inc(S["sdtr"], 1)
                        for di_ in range(16):
                            gd = j * 16 + di_
                            s.wait_ge(S["mmdt"], gd + 1)
                            s.activation(out=dte_t[:, 0:L], in_=dt_ps[:, 0:L],
                                         func=AF.Exp, bias=bdt_sb[:, di_:di_ + 1]
                                         ).then_inc(S["evdt"], 1)
                            if gd >= 2:
                                s.wait_ge(S["odtsp"], 16 * (gd - 1))
                            s.activation(out=dtsp_t[di_ % 2][:, 0:L], in_=dte_t[:, 0:L],
                                         func=AF.Ln, bias=C_EEPS, scale=C_EEPS
                                         ).then_inc(S["sdt"], 1)

                @blk.vector
                def _(v):
                    v.wait_ge(S["c"], 16 * 10 + 2)
                    for j, (c0, L) in enumerate(A_CHUNKS):
                        for e in range(16):
                            v.wait_ge(S["sxin"], j * 16 + e + 1)
                            ge16 = j * 16 + e
                            if ge16 >= 2:
                                v.wait_ge(S["sxc"], ge16 - 1)
                            v.tensor_scalar(
                                xcp_t[e % 2][:, 0:L], xin_t[e % 2][:, 0:L],
                                wc_sb[:, 4 * e:4 * e + 1], bconv_sb[:, e:e + 1],
                                ALU.mult, ALU.add)
                            for k in range(1, 4):
                                inst = v.scalar_tensor_tensor(
                                    xcp_t[e % 2][:, 0:L], xin_t[e % 2][:, k:k + L],
                                    wc_sb[:, 4 * e + k:4 * e + k + 1],
                                    xcp_t[e % 2][:, 0:L], ALU.mult, ALU.add)
                            inst.then_inc(S["vconv"], 1)
                        v.wait_ge(S["mmx"], j + 1)
                        if j >= 2:
                            v.wait_ge(S["obc"], 16 * (j - 1))
                        v.tensor_copy(bch_t[j % 2][64:96, 0:L], dbl_ps[RK:RK + 32, 0:L]
                                      ).then_inc(S["evx"], 1)

                @blk.sync
                def _(sy):
                    for j, (c0, L) in enumerate(A_CHUNKS):
                        for e in range(16):
                            sy.wait_ge(S["sxc"], j * 16 + e + 1)
                            dst = bass.AP(xc_d.tensor, e * 128 * T + c0, [[T, 128], [1, L]])
                            sy.dma_start(out=dst, in_=xc_t[e % 2][:, 0:L]
                                         ).then_inc(S["oxc"], 16)
                        for ez in range(16):
                            sy.wait_ge(S["sz"], j * 16 + ez + 1)
                            dst = bass.AP(siluz_d.tensor, ez * 128 * T + c0,
                                          [[T, 128], [1, L]])
                            sy.dma_start(out=dst, in_=z_t[ez % 2][:, 0:L]
                                         ).then_inc(S["oz"], 16)
                        for di_ in range(16):
                            sy.wait_ge(S["sdt"], j * 16 + di_ + 1)
                            dst = bass.AP(dtsp_d.tensor, di_ * 128 * T + c0,
                                          [[T, 128], [1, L]])
                            sy.dma_start(out=dst, in_=dtsp_t[di_ % 2][:, 0:L]
                                         ).then_inc(S["odtsp"], 16)
                        sy.wait_ge(S["evx"], 2 * (j + 1))
                        dst = bass.AP(bc_d.tensor, c0, [[T, 32], [1, L]])
                        sy.dma_start(out=dst, in_=bch_t[j % 2][64:96, 0:L]
                                     ).then_inc(S["obc"], 16)

        def joinA(eng):
            eng.wait_ge(S["oxc"], 16 * 16 * NCH)
            eng.wait_ge(S["oz"], 16 * 16 * NCH)
            eng.wait_ge(S["odtsp"], 16 * 16 * NCH)
            eng.wait_ge(S["obc"], 16 * NCH)

        if upto == "A":
            with nc.Block() as blkf:
                @blkf.scalar
                def _(s):
                    joinA(s)
            return nc

        with ExitStack() as pb:
            EB = pb.enter_context
            xc_s = [EB(nc.sbuf_tensor(f"xc_s{i}", [128, 2, 2147], F32)) for i in range(2)]
            dtsp_s = EB(nc.sbuf_tensor("dtsp_s", [128, 2, 2147], F32))
            slz_s = EB(nc.sbuf_tensor("slz_s", [128, 2, 2048], F32))
            u_t = EB(nc.sbuf_tensor("u_t", [128, 2, 2147], F32))
            Bb = [EB(nc.sbuf_tensor(f"Bb{i}", [128, 2147], F32)) for i in range(3)]
            Cb = [EB(nc.sbuf_tensor(f"Cb{i}", [128, 2048], F32)) for i in range(2)]
            r_t = EB(nc.sbuf_tensor("r_t", [128, 2147], F32))
            dA_t = [EB(nc.sbuf_tensor(f"dA_t{i}", [128, 2147], F32)) for i in range(2)]
            dBu_t = EB(nc.sbuf_tensor("dBu_t", [128, 2147], F32))
            h_t = EB(nc.sbuf_tensor("h_t", [128, 2147], F32))
            hc_t = [EB(nc.sbuf_tensor(f"hc_t{i}", [128, 2048], BF16)) for i in range(2)]
            t1_t = EB(nc.sbuf_tensor("t1_t", [128, 2048], F32))
            yg_t = [EB(nc.sbuf_tensor(f"yg_t{i}", [128, 2048], F32)) for i in range(2)]
            y_ps = [EB(nc.psum_tensor(f"y_ps{i}", [128, 2048], F32)) for i in range(2)]

            NPS = NPS_GLOBAL

            def slab_base(ps):
                return 16 * 32 * ps

            with nc.Block() as blk:
                @blk.sync
                def _(sy):
                    joinA(sy)
                    for ps in range(NPS):
                        p, sg = ps // 2, ps % 2
                        s0, sl, rl0, rln, rg0 = B_SEGS[sg]
                        for n in range(NST):
                            q2 = ps * 16 + n
                            if q2 >= 3:
                                sy.wait_ge(S["vhc"], 2 * (q2 - 2))
                            src = bass.AP(bc_d.tensor, n * T + s0, [[0, 128], [1, sl]])
                            sy.dma_start(out=Bb[q2 % 3][:, 0:sl], in_=src
                                         ).then_inc(S["slab"], 16)
                            if q2 >= 2:
                                sy.wait_ge(S["vhc"], 2 * (q2 - 1))
                            src = bass.AP(bc_d.tensor, (NST + n) * T + s0 + rl0,
                                          [[0, 128], [1, rln]])
                            sy.dma_start(out=Cb[q2 % 2][:, 0:rln], in_=src
                                         ).then_inc(S["slab"], 16)

                @blk.gpsimd
                def _(g):
                    joinA(g)
                    for ps in range(NPS):
                        p, sg = ps // 2, ps % 2
                        s0, sl, rl0, rln, rg0 = B_SEGS[sg]
                        if ps >= 1:
                            g.wait_ge(S["sda"], 32 * ps)
                            g.wait_ge(S["vscan"], 32 * ps)
                        for m in range(2):
                            db = 2 * p + m
                            src = bass.AP(dtsp_d.tensor, db * 128 * T + s0,
                                          [[T, 128], [1, sl]])
                            g.dma_start(out=dtsp_s[:, m, 0:sl], in_=src
                                        ).then_inc(S["gslab"], 16)
                        if ps >= 1:
                            g.wait_ge(S["gate"], 2 * ps)
                        for m in range(2):
                            db = 2 * p + m
                            src = bass.AP(siluz_d.tensor, db * 128 * T + s0 + rl0,
                                          [[T, 128], [1, rln]])
                            g.dma_start(out=slz_s[:, m, 0:rln], in_=src
                                        ).then_inc(S["gslab"], 16)
                        for m in range(2):
                            db = 2 * p + m
                            g.wait_ge(S["gate"], 2 * ps + m + 1)
                            dst = bass.AP(yg_d.tensor, db * 128 * TR + rg0,
                                          [[TR, 128], [1, rln]])
                            g.dma_start(out=dst, in_=yg_t[m][:, 0:rln]
                                        ).then_inc(S["oyg"], 16)

                @blk.scalar
                def _(s):
                    s.wait_ge(S["oxc"], 16 * 16 * NCH)
                    for m0 in range(2):
                        src = bass.AP(xc_d.tensor, m0 * 128 * T + B_SEGS[0][0],
                                      [[T, 128], [1, B_SEGS[0][1]]])
                        s.dma_start(out=xc_s[0][:, m0, 0:B_SEGS[0][1]], in_=src
                                    ).then_inc(S["xcl"], 16)
                    for ps in range(NPS):
                        p, sg = ps // 2, ps % 2
                        s0, sl, rl0, rln, rg0 = B_SEGS[sg]
                        if ps + 1 < NPS:
                            pn, sgn = (ps + 1) // 2, (ps + 1) % 2
                            s0n, sln = B_SEGS[sgn][0], B_SEGS[sgn][1]
                            if ps >= 1:
                                s.wait_ge(S["gate"], 2 * ps)
                            for m in range(2):
                                db = 2 * pn + m
                                src = bass.AP(xc_d.tensor, db * 128 * T + s0n,
                                              [[T, 128], [1, sln]])
                                s.dma_start(out=xc_s[(ps + 1) % 2][:, m, 0:sln], in_=src
                                            ).then_inc(S["xcl"], 16)
                        s.wait_ge(S["gslab"], 16 * (4 * ps + 2))
                        for n in range(NST):
                            for m in range(2):
                                db = 2 * p + m
                                q = (ps * 16 + n) * 2 + m
                                col = db * 16 + n
                                s.activation(out=r_t[:, 0:sl],
                                             in_=dtsp_s[:, m, 0:sl],
                                             func=AF.Relu, scale=C_M1,
                                             bias=thr[:, col:col + 1])
                                if q >= 2:
                                    s.wait_ge(S["vscan"], q - 1)
                                s.activation(out=dA_t[q % 2][:, 0:sl], in_=r_t[:, 0:sl],
                                             func=AF.Exp, scale=absA[:, col:col + 1],
                                             bias=C_N10).then_inc(S["sda"], 1)

                @blk.vector
                def _(v):
                    for ps in range(NPS):
                        p, sg = ps // 2, ps % 2
                        s0, sl, rl0, rln, rg0 = B_SEGS[sg]
                        v.wait_ge(S["xcl"], 32 * (ps + 1))
                        v.wait_ge(S["gslab"], 16 * (4 * ps + 2))
                        for m in range(2):
                            v.tensor_mul(u_t[:, m, 0:sl], dtsp_s[:, m, 0:sl],
                                         xc_s[ps % 2][:, m, 0:sl])
                        for n in range(NST):
                            q2 = ps * 16 + n
                            v.wait_ge(S["slab"], slab_base(ps) + 32 * (n + 1))
                            if sg == 0:
                                v.tensor_mul(Bb[q2 % 3][:, 0:HALO],
                                             Bb[q2 % 3][:, 0:HALO], bmask_bc[:])
                            for m in range(2):
                                q = q2 * 2 + m
                                v.tensor_mul(dBu_t[:, 0:sl], u_t[:, m, 0:sl],
                                             Bb[q2 % 3][:, 0:sl])
                                v.wait_ge(S["sda"], q + 1)
                                if sg == 1:
                                    v.wait_ge(S["sstate"], 32 * p + 2 * n + m + 1)
                                init = (0.0 if sg == 0
                                        else state_sb[:, m * 16 + n:m * 16 + n + 1])
                                v.tensor_tensor_scan(
                                    h_t[:, 0:sl], dA_t[q % 2][:, 0:sl], dBu_t[:, 0:sl],
                                    init, ALU.mult, ALU.add).then_inc(S["vscan"], 1)
                                if sg == 0:
                                    v.wait_ge(S["vscan"], q + 1)
                                    v.tensor_copy(state_sb[:, m * 16 + n:m * 16 + n + 1],
                                                  h_t[:, sl - 1:sl]).then_inc(S["sstate"], 1)
                                if q >= 2:
                                    v.wait_ge(S["mmb"], q - 1)
                                v.tensor_mul(hc_t[q % 2][:, 0:rln],
                                             h_t[:, rl0:rl0 + rln],
                                             Cb[q2 % 2][:, 0:rln]).then_inc(S["vhc"], 1)
                        v.wait_ge(S["gslab"], 16 * (4 * ps + 4))
                        for m in range(2):
                            db = 2 * p + m
                            v.wait_ge(S["mmb"], ps * 32 + 31 + m)
                            if ps >= 1:
                                v.wait_ge(S["oyg"], 16 * (2 * (ps - 1) + m + 1))
                            v.scalar_tensor_tensor(
                                t1_t[:, 0:rln], xc_s[ps % 2][:, m, rl0:rl0 + rln],
                                dvec_sb[:, db:db + 1], y_ps[m][:, 0:rln],
                                ALU.mult, ALU.add)
                            v.tensor_mul(yg_t[m][:, 0:rln], t1_t[:, 0:rln],
                                         slz_s[:, m, 0:rln]).then_inc(S["gate"], 1)

                @blk.tensor
                def _(t):
                    for ps in range(NPS):
                        for n in range(NST):
                            for m in range(2):
                                q = (ps * 16 + n) * 2 + m
                                if n == 0 and ps >= 1:
                                    t.wait_ge(S["gate"], 2 * (ps - 1) + m + 1)
                                t.wait_ge(S["vhc"], q + 1)
                                for qq in range(4):
                                    inst = t.matmul(
                                        y_ps[m][:, qq * 512:(qq + 1) * 512],
                                        ident_sb[:],
                                        hc_t[q % 2][:, qq * 512:(qq + 1) * 512],
                                        start=(n == 0), stop=(n == 15))
                                inst.then_inc(S["mmb"], 1)

        def joinB(eng):
            eng.wait_ge(S["oyg"], 16 * 2 * NPS_GLOBAL)

        if upto == "B":
            with nc.Block() as blkf:
                @blkf.scalar
                def _(s):
                    joinB(s)
            return nc

        with ExitStack() as pc:
            EC = pc.enter_context
            w_outT_bf = EC(nc.sbuf_tensor("w_outT_bf", [128, 16, DM], BF16))
            w_f1_bf = EC(nc.sbuf_tensor("w_f1_bf", [128, 8, DM], BF16))
            w_f2_bf = EC(nc.sbuf_tensor("w_f2_bf", [128, 8, DM], BF16))
            w_f3_bf = EC(nc.sbuf_tensor("w_f3_bf", [128, 8, DM], BF16))
            yg_sb = EC(nc.sbuf_tensor("yg_sb", [128, 16, LC], F32))
            yn_bf = EC(nc.sbuf_tensor("yn_bf", [128, 16, LC], BF16))
            xres = EC(nc.sbuf_tensor("xres", [128, 8, LC], F32))
            r1_sb = EC(nc.sbuf_tensor("r1_sb", [128, 8, LC], F32))
            ln1_sb = EC(nc.sbuf_tensor("ln1_sb", [128, 8, LC], F32))
            ln1_bf = EC(nc.sbuf_tensor("ln1_bf", [128, 8, LC], BF16))
            h1_bf = yn_bf
            h2_bf = ln1_bf
            out_sb = ln1_sb
            sq_t = [EC(nc.sbuf_tensor(f"sq_t{i}", [128, LC], F32)) for i in range(2)]
            sqr_t = [EC(nc.sbuf_tensor(f"sqr_t{i}", [128, LC], F32)) for i in range(2)]
            th_t = EC(nc.sbuf_tensor("th_t", [128, LC], F32))
            stat_sb = EC(nc.sbuf_tensor("stat_sb", [1, 4 * LC], F32))
            att_ps = EC(nc.psum_tensor("att_ps", [128, LC], F32))
            ffn_ps = EC(nc.psum_tensor("ffn_ps", [128, LC], F32))
            m_ps = [EC(nc.psum_tensor(f"m_ps{i}", [1, LC], F32)) for i in range(2)]
            ms_ps = [EC(nc.psum_tensor(f"ms_ps{i}", [1, LC], F32)) for i in range(2)]
            bc_ps = [EC(nc.psum_tensor(f"bc_ps{i}", [128, LC], F32)) for i in range(2)]

            RSTD0, MEAN0, VAR0, MR0 = 0, LC, 2 * LC, 3 * LC
            rstd_ap = stat_sb[:, RSTD0:RSTD0 + LC]
            mean_ap = stat_sb[:, MEAN0:MEAN0 + LC]
            var_ap = stat_sb[:, VAR0:VAR0 + LC]
            mr_ap = stat_sb[:, MR0:MR0 + LC]

            def yg_src(cc):
                return bass.AP(yg_d.tensor, cc * LC,
                               [[TR, 128], [128 * TR, 16], [1, LC]])

            def xres_src(cc):
                return bass.AP(xT_in.tensor, HALO + cc * LC,
                               [[T, 128], [128 * T, 8], [1, LC]])

            with nc.Block() as blk:
                @blk.gpsimd
                def _(g):
                    g.wait_ge(S["gate"], 32)
                    g.wait_ge(S["slab"], 16 * 32 * NPS_GLOBAL)
                    g.wait_ge(S["xcl"], 32 * NPS_GLOBAL)
                    g.wait_ge(S["gslab"], 16 * 4 * NPS_GLOBAL)
                    src = bass.AP(w_outT.tensor, 0, [[DM, 128], [128 * DM, 16], [1, DM]])
                    g.dma_start(out=w_outT_bf[:], in_=src).then_inc(S["gwc"], 16)
                    for w_, t_ in [(w_f1, w_f1_bf), (w_f2, w_f2_bf), (w_f3, w_f3_bf)]:
                        src = bass.AP(w_.tensor, 0, [[DM, 128], [128 * DM, 8], [1, DM]])
                        g.dma_start(out=t_[:], in_=src).then_inc(S["gwc"], 16)

                @blk.sync
                def _(sy):
                    joinB(sy)
                    for cc in range(NCC):
                        for o in range(8):
                            sy.wait_ge(S["vout"], cc * 8 + o + 1)
                            dst = bass.AP(out_d.tensor, o * 128 * TR + cc * LC,
                                          [[TR, 128], [1, LC]])
                            sy.dma_start(out=dst, in_=out_sb[:, o, :]
                                         ).then_inc(S["oout"], 16)

                @blk.scalar
                def _(s):
                    joinB(s)
                    s.dma_start(out=yg_sb[:], in_=yg_src(0)).then_inc(S["iyg"], 16)
                    s.dma_start(out=xres[:], in_=xres_src(0)).then_inc(S["ixr"], 16)
                    s.wait_ge(S["gwc"], 64)
                    s.wait_ge(S["iyg"], 16)
                    for k in range(16):
                        if k >= 2:
                            s.wait_ge(S["msqr"], k - 1)
                        s.activation(out=sqr_t[k % 2][:], in_=yg_sb[:, k, :],
                                     func=AF.Square).then_inc(S["sqr"], 1)
                    s.wait_ge(S["msqr"], 16)
                    s.activation(out=rstd_ap, in_=m_ps[0][:],
                                 func=AF.Ln, scale=C1_IDI, bias=C1_RMSE)
                    s.activation(out=rstd_ap, in_=rstd_ap,
                                 func=AF.Exp, scale=C1_MH).then_inc(S["srstd_r"], 1)
                    for cc in range(NCC):
                        par = cc % 2
                        for o in range(8):
                            s.wait_ge(S["vr1"], cc * 8 + o + 1)
                            if cc * 8 + o >= 2:
                                s.wait_ge(S["msql1"], cc * 8 + o - 1)
                            if cc >= 1 and o < 2:
                                s.wait_ge(S["msql2"], 8 * cc)
                            s.activation(out=sq_t[o % 2][:], in_=r1_sb[:, o, :],
                                         func=AF.Square).then_inc(S["sql1"], 1)
                        s.wait_ge(S["vstat"], 4 * cc + 1)
                        s.activation(out=var_ap, in_=var_ap, func=AF.Ln, bias=C1_LNE)
                        s.activation(out=var_ap, in_=var_ap,
                                     func=AF.Exp, scale=C1_MH).then_inc(S["srstd_1"], 1)
                        if cc + 1 < NCC:
                            s.wait_ge(S["vyn"], 16 * (cc + 1))
                            s.wait_ge(S["sqr"], 16 * (cc + 1))
                            s.dma_start(out=yg_sb[:], in_=yg_src(cc + 1)
                                        ).then_inc(S["iyg"], 16)
                            s.wait_ge(S["iyg"], 16 * (cc + 2))
                            for k in range(16):
                                gi = (cc + 1) * 16 + k
                                s.wait_ge(S["msqr"], gi - 1)
                                s.activation(out=sqr_t[gi % 2][:], in_=yg_sb[:, k, :],
                                             func=AF.Square).then_inc(S["sqr"], 1)
                            s.wait_ge(S["msqr"], 16 * (cc + 2))
                            s.wait_ge(S["mbcr"], cc + 1)
                            s.activation(out=rstd_ap, in_=m_ps[par ^ 1][:],
                                         func=AF.Ln, scale=C1_IDI, bias=C1_RMSE)
                            s.activation(out=rstd_ap, in_=rstd_ap,
                                         func=AF.Exp, scale=C1_MH).then_inc(S["srstd_r"], 1)
                        if cc + 1 < NCC:
                            s.wait_ge(S["vr1"], 8 * (cc + 1))
                            s.dma_start(out=xres[:], in_=xres_src(cc + 1)
                                        ).then_inc(S["ixr"], 16)
                        for o in range(8):
                            s.wait_ge(S["mf"], cc * 24 + o + 1)
                            s.copy(h1_bf[:, o, :], (ffn_ps if o % 2 == 0 else att_ps)[:]
                                   ).then_inc(S["sf"], 1)
                        for o in range(8):
                            s.wait_ge(S["mf"], cc * 24 + 8 + o + 1)
                            s.copy(h2_bf[:, o, :], (ffn_ps if o % 2 == 0 else att_ps)[:]
                                   ).then_inc(S["sf"], 1)
                        for o in range(8):
                            s.wait_ge(S["mf"], cc * 24 + 16 + o + 1)
                            if cc * 8 + o >= 1:
                                s.wait_ge(S["vr2"], cc * 8 + o)
                            s.activation(out=th_t[:],
                                         in_=(ffn_ps if o % 2 == 0 else att_ps)[:],
                                         func=AF.Tanh, bias=fbias_sb[:, o:o + 1]
                                         ).then_inc(S["sf"], 1)
                        for o in range(8):
                            s.wait_ge(S["vr2"], cc * 8 + o + 1)
                            if cc * 8 + o >= 2:
                                s.wait_ge(S["msql2"], cc * 8 + o - 1)
                            if o < 2:
                                s.wait_ge(S["msql1"], 8 * (cc + 1))
                            s.activation(out=sq_t[o % 2][:], in_=r1_sb[:, o, :],
                                         func=AF.Square).then_inc(S["sql2"], 1)
                        s.wait_ge(S["vstat"], 4 * cc + 3)
                        s.activation(out=var_ap, in_=var_ap, func=AF.Ln, bias=C1_LNE)
                        s.activation(out=var_ap, in_=var_ap,
                                     func=AF.Exp, scale=C1_MH).then_inc(S["srstd_2"], 1)


                @blk.tensor
                def _(t):
                    t.wait_ge(S["gwc"], 64)
                    for k in range(16):
                        t.wait_ge(S["sqr"], k + 1)
                        t.matmul(m_ps[0][:], ones_col[:], sqr_t[k % 2][:],
                                 start=(k == 0), stop=(k == 15)).then_inc(S["msqr"], 1)
                    t.wait_ge(S["srstd_r"], 1)
                    t.matmul(att_ps[:], ones_row[:], rstd_ap,
                             start=True, stop=True).then_inc(S["mbcr"], 1)
                    for cc in range(NCC):
                        par = cc % 2
                        t.wait_ge(S["vyn"], 16 * (cc + 1))
                        for o in range(8):
                            if cc * 8 + o >= 1:
                                t.wait_ge(S["vr1"], cc * 8 + o)
                            for k in range(16):
                                inst = t.matmul(att_ps[:],
                                                w_outT_bf[:, k, o * 128:(o + 1) * 128],
                                                yn_bf[:, k, :],
                                                start=(k == 0), stop=(k == 15))
                            inst.then_inc(S["matt"], 1)
                        t.wait_ge(S["srstd_r"], cc + 1)
                        if cc >= 1:
                            t.wait_ge(S["vstat"], 4 * (cc - 1) + 3)
                        for o in range(8):
                            t.wait_ge(S["vr1"], cc * 8 + o + 1)
                            inst = t.matmul(m_ps[par][:], ones_col[:], r1_sb[:, o, :],
                                            start=(o == 0), stop=(o == 7))
                        inst.then_inc(S["mstat1"], 1)
                        for o in range(8):
                            t.wait_ge(S["sql1"], cc * 8 + o + 1)
                            t.matmul(ms_ps[par][:], ones_col[:], sq_t[o % 2][:],
                                     start=(o == 0), stop=(o == 7)).then_inc(S["msql1"], 1)
                        t.wait_ge(S["srstd_1"], cc + 1)
                        t.wait_ge(S["vstat"], 4 * cc + 2)
                        if cc >= 1:
                            t.wait_ge(S["vout"], 8 * cc)
                        t.matmul(bc_ps[0][:], ones_row[:], var_ap, start=True, stop=True)
                        t.matmul(bc_ps[1][:], ones_row[:], mr_ap,
                                 start=True, stop=True).then_inc(S["mbc1"], 1)
                        if cc + 1 < NCC:
                            if cc >= 1:
                                t.wait_ge(S["vstat"], 4 * (cc - 1) + 3)
                            for k in range(16):
                                gi = (cc + 1) * 16 + k
                                t.wait_ge(S["sqr"], gi + 1)
                                t.matmul(m_ps[par ^ 1][:], ones_col[:], sqr_t[gi % 2][:],
                                         start=(k == 0), stop=(k == 15)
                                         ).then_inc(S["msqr"], 1)
                        t.wait_ge(S["vln"], 8 * (cc + 1))
                        for ff in range(3):
                            wt = [w_f1_bf, w_f2_bf, w_f3_bf][ff]
                            src_bf = [ln1_bf, h1_bf, h2_bf][ff]
                            if ff > 0:
                                t.wait_ge(S["sf"], cc * 24 + 8 * ff)
                            for o in range(8):
                                gf = cc * 24 + 8 * ff + o
                                if ff == 0 and o == 1:
                                    t.wait_ge(S["vr1"], 8 * (cc + 1))
                                if gf >= 2:
                                    t.wait_ge(S["sf"], gf - 1)
                                bank = ffn_ps if o % 2 == 0 else att_ps
                                for k in range(8):
                                    inst = t.matmul(bank[:],
                                                    wt[:, k, o * 128:(o + 1) * 128],
                                                    src_bf[:, k, :],
                                                    start=(k == 0), stop=(k == 7))
                                inst.then_inc(S["mf"], 1)
                        if cc + 1 < NCC:
                            t.wait_ge(S["srstd_r"], cc + 2)
                            t.wait_ge(S["sf"], 24 * (cc + 1))
                            t.matmul(att_ps[:], ones_row[:], rstd_ap,
                                     start=True, stop=True).then_inc(S["mbcr"], 1)
                        t.wait_ge(S["vstat"], 4 * cc + 1)
                        for o in range(8):
                            t.wait_ge(S["vr2"], cc * 8 + o + 1)
                            inst = t.matmul(m_ps[par][:], ones_col[:], r1_sb[:, o, :],
                                            start=(o == 0), stop=(o == 7))
                        inst.then_inc(S["mstat2"], 1)
                        for o in range(8):
                            t.wait_ge(S["sql2"], cc * 8 + o + 1)
                            t.matmul(ms_ps[par][:], ones_col[:], sq_t[o % 2][:],
                                     start=(o == 0), stop=(o == 7)).then_inc(S["msql2"], 1)
                        t.wait_ge(S["srstd_2"], cc + 1)
                        t.wait_ge(S["vstat"], 4 * cc + 4)
                        t.wait_ge(S["vln"], 8 * (cc + 1))
                        t.matmul(bc_ps[0][:], ones_row[:], var_ap, start=True, stop=True)
                        t.matmul(bc_ps[1][:], ones_row[:], mr_ap,
                                 start=True, stop=True).then_inc(S["mbc2"], 1)

                @blk.vector
                def _(v):
                    def emit_yn(cc):
                        v.wait_ge(S["mbcr"], cc + 1)
                        v.wait_ge(S["iyg"], 16 * (cc + 1))
                        for k in range(16):
                            v.scalar_tensor_tensor(
                                yn_bf[:, k, :], yg_sb[:, k, :], rmsw_sb[:, k:k + 1],
                                att_ps[:], ALU.mult, ALU.mult).then_inc(S["vyn"], 1)

                    emit_yn(0)
                    for cc in range(NCC):
                        par = cc % 2
                        v.wait_ge(S["ixr"], 16 * (cc + 1))
                        for o in range(8):
                            v.wait_ge(S["matt"], cc * 8 + o + 1)
                            v.tensor_add(r1_sb[:, o, :], att_ps[:],
                                         xres[:, o, :]).then_inc(S["vr1"], 1)
                        v.wait_ge(S["mstat1"], cc + 1)
                        v.tensor_scalar_mul(mean_ap, m_ps[par][:], 1.0 / DM)
                        v.tensor_mul(mr_ap, mean_ap, mean_ap)
                        v.wait_ge(S["msql1"], 8 * (cc + 1))
                        v.scalar_tensor_tensor(var_ap, ms_ps[par][:], 1.0 / DM, mr_ap,
                                               ALU.mult, ALU.subtract
                                               ).then_inc(S["vstat"], 1)
                        v.wait_ge(S["srstd_1"], cc + 1)
                        v.tensor_mul(mr_ap, mean_ap, var_ap).then_inc(S["vstat"], 1)
                        v.wait_ge(S["mbc1"], cc + 1)
                        if cc >= 1:
                            v.wait_ge(S["oout"], 16 * 8 * cc)
                        for o in range(8):
                            v.tensor_mul(ln1_sb[:, o, :], r1_sb[:, o, :], bc_ps[0][:])
                            v.tensor_sub(ln1_sb[:, o, :], ln1_sb[:, o, :], bc_ps[1][:])
                            v.tensor_copy(ln1_bf[:, o, :], ln1_sb[:, o, :]
                                          ).then_inc(S["vln"], 1)
                        v.wait_ge(S["mstat1"], cc + 1)
                        for o in range(8):
                            v.wait_ge(S["sf"], cc * 24 + 16 + o + 1)
                            v.wait_ge(S["sql1"], cc * 8 + o + 1)
                            v.tensor_add(r1_sb[:, o, :], ln1_sb[:, o, :],
                                         th_t[:]).then_inc(S["vr2"], 1)
                        v.wait_ge(S["mstat2"], cc + 1)
                        v.tensor_scalar_mul(mean_ap, m_ps[par][:], 1.0 / DM)
                        v.tensor_mul(mr_ap, mean_ap, mean_ap)
                        v.wait_ge(S["msql2"], 8 * (cc + 1))
                        v.scalar_tensor_tensor(var_ap, ms_ps[par][:], 1.0 / DM, mr_ap,
                                               ALU.mult, ALU.subtract
                                               ).then_inc(S["vstat"], 1)
                        v.wait_ge(S["srstd_2"], cc + 1)
                        v.tensor_mul(mr_ap, mean_ap, var_ap).then_inc(S["vstat"], 1)
                        if cc + 1 < NCC:
                            emit_yn(cc + 1)
                        v.wait_ge(S["mbc2"], cc + 1)
                        for o in range(8):
                            v.tensor_mul(out_sb[:, o, :], r1_sb[:, o, :], bc_ps[0][:])
                            v.tensor_sub(out_sb[:, o, :], out_sb[:, o, :], bc_ps[1][:]
                                         ).then_inc(S["vout"], 1)

        with nc.Block() as blkf:
            @blkf.scalar
            def _(s):
                s.wait_ge(S["oout"], 16 * 8 * NCC)
        return nc


def _layout_inputs(inputs):
    import ml_dtypes
    x = np.asarray(inputs["L0_attn_x"], np.float32)

    def c(a):
        return np.ascontiguousarray(a)

    shared = dict(
        w_inT=c(np.asarray(inputs["L0_attn_W_in"], np.float32).T),
        w_xT=c(np.asarray(inputs["L0_attn_W_x"], np.float32).T),
        w_dtT=c(np.asarray(inputs["L0_attn_W_dt"], np.float32).T),
        w_outT=c(np.asarray(inputs["L0_attn_W_out"], np.float32).T),
        w_f1=c(np.asarray(inputs["L0_ffn_weight"], np.float32)),
        w_f2=c(np.asarray(inputs["L0_ffn_aux_weight1"], np.float32)),
        w_f3=c(np.asarray(inputs["L0_ffn_aux_weight2"], np.float32)),
        a_re=c(np.asarray(inputs["L0_attn_A_log"], np.float32)
               .reshape(16, 128, NST).transpose(1, 0, 2).reshape(128, 256)),
        wc_re=c(np.asarray(inputs["L0_attn_W_conv"], np.float32)
                .reshape(16, 128, 4).transpose(1, 0, 2).reshape(128, 64)),
        bconv_re=c(np.asarray(inputs["L0_attn_b_conv"], np.float32).reshape(16, 128).T),
        bdt_re=c(np.asarray(inputs["L0_attn_b_dt"], np.float32).reshape(16, 128).T),
        d_re=c(np.asarray(inputs["L0_attn_D"], np.float32).reshape(16, 128).T),
        rmsw_re=c(np.asarray(inputs["L0_attn_rms_w"], np.float32).reshape(16, 128).T),
        fbias_re=c(np.asarray(inputs["L0_ffn_bias"], np.float32).reshape(8, 128).T),
        ident_in=np.eye(128, dtype=ml_dtypes.bfloat16),
        consts_in=np.tile(np.array([[E_EPS, -10.0, -1.0, 1.0 / DI, RMSEPS,
                                     -0.5, LNEPS, 0.0]], np.float32), (128, 1)),
    )
    in_maps = []
    for core in range(NCORES):
        b, hf = core // 2, core % 2
        xT = np.zeros((DM, T), np.float32)
        s0 = hf * TR - HALO
        if s0 < 0:
            xT[:, HALO:] = x[b, 0:TR].T
            bmask = np.zeros((1, HALO), np.float32)
        else:
            xT[:, :] = x[b, s0:hf * TR + TR].T
            bmask = np.ones((1, HALO), np.float32)
        m = dict(shared)
        m["xT_in"] = xT
        m["bmask_in"] = bmask
        in_maps.append(m)
    return in_maps


_CACHE = {}


def kernel(**inputs):
    in_maps = _layout_inputs(inputs)
    if "nc" not in _CACHE:
        _CACHE["nc"] = build_nc("C")
    nc = _CACHE["nc"]
    res = run_bass_kernel_spmd(nc, in_maps, core_ids=list(range(NCORES)))
    out = np.empty((4, 8192, DM), np.float32)
    for core in range(NCORES):
        b, hf = core // 2, core % 2
        out[b, hf * TR:(hf + 1) * TR, :] = res.results[core]["out_d"].T
    return out


# version 10
# speedup vs baseline: 1.0086x; 1.0086x over previous
import numpy as np
from contextlib import ExitStack

import concourse.bass as bass
import concourse.mybir as mybir
from concourse.bass_utils import run_bass_kernel_spmd

F32 = mybir.dt.float32
BF16 = mybir.dt.bfloat16
AF = mybir.ActivationFunctionType
ALU = mybir.AluOpType

NCORES = 8
DM = 1024
DI = 2048
NST = 16
RK = 64
HALO = 99
TR = 4096
T = TR + HALO
E_EPS = float(np.exp(np.float64(1e-4)))
LNEPS = 1e-5
RMSEPS = 1e-6

import os
A_CHUNKS = [(0, 1024), (1024, 1024), (2048, 1024), (3072, 1024), (4096, HALO)]
if os.environ.get("DEV_ACHUNKS"):
    A_CHUNKS = A_CHUNKS[: int(os.environ["DEV_ACHUNKS"])]
NCH = len(A_CHUNKS)
NPS_GLOBAL = 16
if os.environ.get("DEV_BPAIRS"):
    NPS_GLOBAL = 2 * int(os.environ["DEV_BPAIRS"])
B_SEGS = [(0, 2147, 99, 2048, 0), (2147, 2048, 0, 2048, 2048)]
LC = 512
NCC = TR // LC


def _halves(L):
    return [(h, min(512, L - h)) for h in range(0, L, 512)]


def build_nc(upto="C"):
    nc = bass.Bass("TRN2", target_bir_lowering=False,
                   detect_race_conditions=False)

    def din(name, shape, dtype=F32):
        return nc.dram_tensor(name, shape, dtype, kind="ExternalInput").ap()

    xT_in = din("xT_in", [DM, T])
    w_inT = din("w_inT", [DM, 2 * DI])
    w_xT = din("w_xT", [DI, RK + 2 * NST])
    w_dtT = din("w_dtT", [RK, DI])
    w_outT = din("w_outT", [DI, DM])
    w_f1 = din("w_f1", [DM, DM])
    w_f2 = din("w_f2", [DM, DM])
    w_f3 = din("w_f3", [DM, DM])
    a_re = din("a_re", [128, NST * NST])
    wc_re = din("wc_re", [128, 16 * 4])
    bconv_re = din("bconv_re", [128, 16])
    bdt_re = din("bdt_re", [128, 16])
    d_re = din("d_re", [128, 16])
    rmsw_re = din("rmsw_re", [128, 16])
    fbias_re = din("fbias_re", [128, 8])
    bmask_in = din("bmask_in", [1, HALO])
    consts_in = din("consts_in", [128, 8])
    ident_in = din("ident_in", [128, 128], BF16)

    def dscr(name, shape, out=False):
        kind = "ExternalOutput" if out else "Internal"
        return nc.dram_tensor(name, shape, F32, kind=kind).ap()

    xc_d = dscr("xc_d", [DI, T], out=(upto == "A"))
    siluz_d = dscr("siluz_d", [DI, T], out=(upto == "A"))
    dtsp_d = dscr("dtsp_d", [DI, T], out=(upto == "A"))
    bc_d = dscr("bc_d", [2 * NST, T], out=(upto == "A"))
    yg_d = dscr("yg_d", [DI, TR], out=(upto == "B"))
    out_d = dscr("out_d", [DM, TR], out=(upto == "C"))

    with ExitStack() as outer:
        E = outer.enter_context
        absA = E(nc.sbuf_tensor("absA", [128, NST * NST], F32))
        thr = E(nc.sbuf_tensor("thr", [128, NST * NST], F32))
        wc_sb = E(nc.sbuf_tensor("wc_sb", [128, 64], F32))
        bconv_sb = E(nc.sbuf_tensor("bconv_sb", [128, 16], F32))
        bdt_sb = E(nc.sbuf_tensor("bdt_sb", [128, 16], F32))
        dvec_sb = E(nc.sbuf_tensor("dvec_sb", [128, 16], F32))
        rmsw_sb = E(nc.sbuf_tensor("rmsw_sb", [128, 16], F32))
        fbias_sb = E(nc.sbuf_tensor("fbias_sb", [128, 8], F32))
        bmask_bc = E(nc.sbuf_tensor("bmask_bc", [128, HALO], F32))
        consts_sb = E(nc.sbuf_tensor("consts_sb", [128, 8], F32))
        C_EEPS = consts_sb[:, 0:1]
        C_N10 = consts_sb[:, 1:2]
        C_M1 = consts_sb[:, 2:3]
        C_IDI = consts_sb[:, 3:4]
        C_RMSE = consts_sb[:, 4:5]
        C_MH = consts_sb[:, 5:6]
        C_LNE = consts_sb[:, 6:7]
        C1_IDI = consts_sb[0:1, 3:4]
        C1_RMSE = consts_sb[0:1, 4:5]
        C1_MH = consts_sb[0:1, 5:6]
        C1_LNE = consts_sb[0:1, 6:7]
        ident_sb = E(nc.sbuf_tensor("ident_sb", [128, 128], BF16))
        ones_col = E(nc.sbuf_tensor("ones_col", [128, 1], F32))
        ones_row = E(nc.sbuf_tensor("ones_row", [1, 128], F32))
        state_sb = E(nc.sbuf_tensor("state_sb", [128, 32], F32))

        S = {}
        for s_ in ["c", "gw", "gx", "mma", "eva", "sxin", "vconv", "sxc", "gxc",
                   "oxc", "oz", "sz", "mmx", "evx", "sdtr", "mmdt", "evdt",
                   "sdt", "odtsp", "obc",
                   "slab", "gslab", "xcl", "sda", "vscan", "vhc", "mmb", "gate", "oyg", "sstate",
                   "gwc", "iyg", "ixr", "sqr", "msqr", "sql1", "msql1",
                   "sql2", "msql2", "srstd_r", "srstd_1", "srstd_2", "mbcr", "mbc1", "mbc2",
                   "vyn", "matt", "vr1", "mstat1", "mstat2", "vstat", "vln", "gln",
                   "mf", "sf", "vr2", "vout", "oout"]:
            S[s_] = E(nc.semaphore(s_))

        with nc.Block() as blk0:
            @blk0.sync
            def _(sy):
                for t_, src in [(absA, a_re), (wc_sb, wc_re), (bconv_sb, bconv_re),
                                (bdt_sb, bdt_re), (dvec_sb, d_re), (rmsw_sb, rmsw_re),
                                (fbias_sb, fbias_re), (ident_sb, ident_in),
                                (consts_sb, consts_in)]:
                    sy.dma_start(out=t_[:], in_=src).then_inc(S["c"], 16)
                src = bass.AP(bmask_in.tensor, 0, [[0, 128], [1, HALO]])
                sy.dma_start(out=bmask_bc[:], in_=src).then_inc(S["c"], 16)

            @blk0.scalar
            def _(s):
                s.wait_ge(S["c"], 16 * 10)
                s.activation(out=absA[:], in_=absA[:], func=AF.Exp).then_inc(S["c"], 1)

        with nc.Block() as blk0b:
            @blk0b.vector
            def _(v):
                v.memset(ones_col[:], 1.0)
                v.memset(ones_row[:], 1.0)
                v.wait_ge(S["c"], 16 * 10 + 1)
                v.reciprocal(thr[:], absA[:])
                v.tensor_scalar_mul(thr[:], thr[:], 10.0).then_inc(S["c"], 1)

        with ExitStack() as pa:
            EA = pa.enter_context
            w_inT_bf = EA(nc.sbuf_tensor("w_inT_bf", [128, 8, 2 * DI], BF16))
            w_xT_bf = EA(nc.sbuf_tensor("w_xT_bf", [128, 16, RK + 2 * NST], BF16))
            w_dtT_bf = EA(nc.sbuf_tensor("w_dtT_bf", [RK, DI], BF16))
            xT_bf = [EA(nc.sbuf_tensor(f"xT_bf{i}", [128, 8, 1024], BF16)) for i in range(2)]
            xin_t = [EA(nc.sbuf_tensor(f"xin_t{i}", [128, 1027], F32)) for i in range(2)]
            xcp_t = [EA(nc.sbuf_tensor(f"xcp_t{i}", [128, 1024], F32)) for i in range(2)]
            xc_t = [EA(nc.sbuf_tensor(f"xc_t{i}", [128, 1024], F32)) for i in range(2)]
            z_t = [EA(nc.sbuf_tensor(f"z_t{i}", [128, 1024], F32)) for i in range(2)]
            xcbf = EA(nc.sbuf_tensor("xcbf", [128, 16, 1024], BF16))
            dtr_bf = EA(nc.sbuf_tensor("dtr_bf", [RK, 1024], BF16))
            dte_t = EA(nc.sbuf_tensor("dte_t", [128, 1024], F32))
            dtsp_t = [EA(nc.sbuf_tensor(f"dtsp_t{i}", [128, 1024], F32)) for i in range(2)]
            bch_t = [EA(nc.sbuf_tensor(f"bch_t{i}", [96, 1024], F32)) for i in range(2)]
            tails = EA(nc.sbuf_tensor("tails", [128, 48], F32))
            xz_ps = [EA(nc.psum_tensor(f"xz_ps{i}", [128, 1024], F32)) for i in range(2)]
            dbl_ps = EA(nc.psum_tensor("dbl_ps", [RK + 2 * NST, 1024], F32))
            dt_ps = EA(nc.psum_tensor("dt_ps", [128, 1024], F32))

            with nc.Block() as blk:
                @blk.gpsimd
                def _(g):
                    src = bass.AP(w_inT.tensor, 0,
                                  [[2 * DI, 128], [128 * 2 * DI, 8], [1, 2 * DI]])
                    g.dma_start(out=w_inT_bf[:], in_=src).then_inc(S["gw"], 16)
                    ww = RK + 2 * NST
                    src = bass.AP(w_xT.tensor, 0, [[ww, 128], [128 * ww, 16], [1, ww]])
                    g.dma_start(out=w_xT_bf[:], in_=src).then_inc(S["gw"], 16)
                    g.dma_start(out=w_dtT_bf[:], in_=w_dtT).then_inc(S["gw"], 16)
                    for j, (c0, L) in enumerate(A_CHUNKS):
                        if j >= 2:
                            g.wait_ge(S["mma"], (j - 1) * 32)
                        src = bass.AP(xT_in.tensor, c0, [[T, 128], [128 * T, 8], [1, L]])
                        g.dma_start(out=xT_bf[j % 2][:, :, 0:L], in_=src
                                    ).then_inc(S["gx"], 16)
                        if j >= 1:
                            g.wait_ge(S["mmx"], j)
                        for e in range(16):
                            g.wait_ge(S["sxc"], j * 16 + e + 1)
                            g.dma_start(out=xcbf[:, e, 0:L], in_=xc_t[e % 2][:, 0:L]
                                        ).then_inc(S["gxc"], 16)

                @blk.tensor
                def _(t):
                    t.wait_ge(S["gw"], 48)
                    for j, (c0, L) in enumerate(A_CHUNKS):
                        t.wait_ge(S["gx"], 16 * (j + 1))
                        for e in range(32):
                            ge = j * 32 + e
                            if ge >= 2:
                                t.wait_ge(S["eva"], ge - 1)
                            for h0, hl in _halves(L):
                                for k in range(8):
                                    inst = t.matmul(
                                        xz_ps[e % 2][:, h0:h0 + hl],
                                        w_inT_bf[:, k, e * 128:(e + 1) * 128],
                                        xT_bf[j % 2][:, k, h0:h0 + hl],
                                        start=(k == 0), stop=(k == 7))
                            inst.then_inc(S["mma"], 1)
                        t.wait_ge(S["gxc"], 256 * (j + 1))
                        if j >= 1:
                            t.wait_ge(S["evx"], 2 * j)
                        for h0, hl in _halves(L):
                            for k in range(16):
                                inst = t.matmul(
                                    dbl_ps[:, h0:h0 + hl], w_xT_bf[:, k, :],
                                    xcbf[:, k, h0:h0 + hl],
                                    start=(k == 0), stop=(k == 15))
                        inst.then_inc(S["mmx"], 1)
                        t.wait_ge(S["sdtr"], j + 1)
                        for di_ in range(16):
                            gd = j * 16 + di_
                            if gd >= 1:
                                t.wait_ge(S["evdt"], gd)
                            for h0, hl in _halves(L):
                                inst = t.matmul(
                                    dt_ps[:, h0:h0 + hl],
                                    w_dtT_bf[:, di_ * 128:(di_ + 1) * 128],
                                    dtr_bf[:, h0:h0 + hl],
                                    start=True, stop=True)
                            inst.then_inc(S["mmdt"], 1)

                @blk.scalar
                def _(s):
                    def emit_silu(j, L, e):
                        s.wait_ge(S["vconv"], j * 16 + e + 1)
                        ge16 = j * 16 + e
                        if ge16 >= 2:
                            s.wait_ge(S["oxc"], 16 * (ge16 - 1))
                            s.wait_ge(S["gxc"], 16 * (ge16 - 1))
                        s.activation(out=xc_t[e % 2][:, 0:L],
                                     in_=xcp_t[e % 2][:, 0:L], func=AF.Silu
                                     ).then_inc(S["sxc"], 1)

                    for j, (c0, L) in enumerate(A_CHUNKS):
                        for e in range(32):
                            s.wait_ge(S["mma"], j * 32 + e + 1)
                            if e < 16:
                                ge16 = j * 16 + e
                                if ge16 >= 2:
                                    s.wait_ge(S["vconv"], ge16 - 1)
                                if j > 0:
                                    s.copy(xin_t[e % 2][:, 0:3], tails[:, 3 * e:3 * e + 3])
                                else:
                                    s.memzero(xin_t[e % 2][:, 0:3])
                                s.copy(xin_t[e % 2][:, 3:3 + L], xz_ps[e % 2][:, 0:L]
                                       ).then_inc(S["eva"], 1)
                                if j < NCH - 1:
                                    s.copy(tails[:, 3 * e:3 * e + 3],
                                           xin_t[e % 2][:, L:L + 3])
                                s.copy(dte_t[:, 0:1], xin_t[e % 2][:, 0:1]
                                       ).then_inc(S["sxin"], 1)
                            else:
                                ez = e - 16
                                gz = j * 16 + ez
                                if gz >= 2:
                                    s.wait_ge(S["oz"], 16 * (gz - 1))
                                s.activation(out=z_t[ez % 2][:, 0:L],
                                             in_=xz_ps[e % 2][:, 0:L], func=AF.Silu
                                             ).then_inc(S["eva"], 1)
                                s.copy(dte_t[:, 0:1], z_t[ez % 2][:, 0:1]
                                       ).then_inc(S["sz"], 1)
                            if 2 <= e < 18:
                                emit_silu(j, L, e - 2)
                        s.wait_ge(S["mmx"], j + 1)
                        s.copy(dtr_bf[:, 0:L], dbl_ps[0:RK, 0:L]).then_inc(S["evx"], 1)
                        s.copy(dte_t[0:1, 0:1], dte_t[0:1, 0:1]).then_inc(S["sdtr"], 1)
                        for di_ in range(16):
                            gd = j * 16 + di_
                            s.wait_ge(S["mmdt"], gd + 1)
                            s.activation(out=dte_t[:, 0:L], in_=dt_ps[:, 0:L],
                                         func=AF.Exp, bias=bdt_sb[:, di_:di_ + 1]
                                         ).then_inc(S["evdt"], 1)
                            if gd >= 2:
                                s.wait_ge(S["odtsp"], 16 * (gd - 1))
                            s.activation(out=dtsp_t[di_ % 2][:, 0:L], in_=dte_t[:, 0:L],
                                         func=AF.Ln, bias=C_EEPS, scale=C_EEPS
                                         ).then_inc(S["sdt"], 1)

                @blk.vector
                def _(v):
                    v.wait_ge(S["c"], 16 * 10 + 2)
                    for j, (c0, L) in enumerate(A_CHUNKS):
                        for e in range(16):
                            v.wait_ge(S["sxin"], j * 16 + e + 1)
                            ge16 = j * 16 + e
                            if ge16 >= 2:
                                v.wait_ge(S["sxc"], ge16 - 1)
                            v.tensor_scalar(
                                xcp_t[e % 2][:, 0:L], xin_t[e % 2][:, 0:L],
                                wc_sb[:, 4 * e:4 * e + 1], bconv_sb[:, e:e + 1],
                                ALU.mult, ALU.add)
                            for k in range(1, 4):
                                inst = v.scalar_tensor_tensor(
                                    xcp_t[e % 2][:, 0:L], xin_t[e % 2][:, k:k + L],
                                    wc_sb[:, 4 * e + k:4 * e + k + 1],
                                    xcp_t[e % 2][:, 0:L], ALU.mult, ALU.add)
                            inst.then_inc(S["vconv"], 1)
                        v.wait_ge(S["mmx"], j + 1)
                        if j >= 2:
                            v.wait_ge(S["obc"], 16 * (j - 1))
                        v.tensor_copy(bch_t[j % 2][64:96, 0:L], dbl_ps[RK:RK + 32, 0:L]
                                      ).then_inc(S["evx"], 1)

                @blk.sync
                def _(sy):
                    for j, (c0, L) in enumerate(A_CHUNKS):
                        for e in range(16):
                            sy.wait_ge(S["sxc"], j * 16 + e + 1)
                            dst = bass.AP(xc_d.tensor, e * 128 * T + c0, [[T, 128], [1, L]])
                            sy.dma_start(out=dst, in_=xc_t[e % 2][:, 0:L]
                                         ).then_inc(S["oxc"], 16)
                        for ez in range(16):
                            sy.wait_ge(S["sz"], j * 16 + ez + 1)
                            dst = bass.AP(siluz_d.tensor, ez * 128 * T + c0,
                                          [[T, 128], [1, L]])
                            sy.dma_start(out=dst, in_=z_t[ez % 2][:, 0:L]
                                         ).then_inc(S["oz"], 16)
                        for di_ in range(16):
                            sy.wait_ge(S["sdt"], j * 16 + di_ + 1)
                            dst = bass.AP(dtsp_d.tensor, di_ * 128 * T + c0,
                                          [[T, 128], [1, L]])
                            sy.dma_start(out=dst, in_=dtsp_t[di_ % 2][:, 0:L]
                                         ).then_inc(S["odtsp"], 16)
                        sy.wait_ge(S["evx"], 2 * (j + 1))
                        dst = bass.AP(bc_d.tensor, c0, [[T, 32], [1, L]])
                        sy.dma_start(out=dst, in_=bch_t[j % 2][64:96, 0:L]
                                     ).then_inc(S["obc"], 16)

        def joinA(eng):
            eng.wait_ge(S["oxc"], 16 * 16 * NCH)
            eng.wait_ge(S["oz"], 16 * 16 * NCH)
            eng.wait_ge(S["odtsp"], 16 * 16 * NCH)
            eng.wait_ge(S["obc"], 16 * NCH)

        if upto == "A":
            with nc.Block() as blkf:
                @blkf.scalar
                def _(s):
                    joinA(s)
            return nc

        with ExitStack() as pb:
            EB = pb.enter_context
            xc_s = [EB(nc.sbuf_tensor(f"xc_s{i}", [128, 2, 2147], F32)) for i in range(2)]
            dtsp_s = EB(nc.sbuf_tensor("dtsp_s", [128, 2, 2147], F32))
            slz_s = EB(nc.sbuf_tensor("slz_s", [128, 2, 2048], F32))
            u_t = EB(nc.sbuf_tensor("u_t", [128, 2, 2147], F32))
            Bb = [EB(nc.sbuf_tensor(f"Bb{i}", [128, 2147], F32)) for i in range(3)]
            Cb = [EB(nc.sbuf_tensor(f"Cb{i}", [128, 2048], F32)) for i in range(2)]
            r_t = EB(nc.sbuf_tensor("r_t", [128, 2147], F32))
            dA_t = [EB(nc.sbuf_tensor(f"dA_t{i}", [128, 2147], F32)) for i in range(2)]
            dBu_t = EB(nc.sbuf_tensor("dBu_t", [128, 2147], F32))
            h_t = EB(nc.sbuf_tensor("h_t", [128, 2147], F32))
            hc_t = [EB(nc.sbuf_tensor(f"hc_t{i}", [128, 2048], BF16)) for i in range(2)]
            t1_t = EB(nc.sbuf_tensor("t1_t", [128, 2048], F32))
            yg_t = [EB(nc.sbuf_tensor(f"yg_t{i}", [128, 2048], F32)) for i in range(2)]
            y_ps = [EB(nc.psum_tensor(f"y_ps{i}", [128, 2048], F32)) for i in range(2)]

            NPS = NPS_GLOBAL

            def slab_base(ps):
                return 16 * 32 * ps

            with nc.Block() as blk:
                @blk.sync
                def _(sy):
                    joinA(sy)
                    for ps in range(NPS):
                        p, sg = ps // 2, ps % 2
                        s0, sl, rl0, rln, rg0 = B_SEGS[sg]
                        for n in range(NST):
                            q2 = ps * 16 + n
                            if q2 >= 3:
                                sy.wait_ge(S["vhc"], 2 * (q2 - 2))
                            src = bass.AP(bc_d.tensor, n * T + s0, [[0, 128], [1, sl]])
                            sy.dma_start(out=Bb[q2 % 3][:, 0:sl], in_=src
                                         ).then_inc(S["slab"], 16)
                            if q2 >= 2:
                                sy.wait_ge(S["vhc"], 2 * (q2 - 1))
                            src = bass.AP(bc_d.tensor, (NST + n) * T + s0 + rl0,
                                          [[0, 128], [1, rln]])
                            sy.dma_start(out=Cb[q2 % 2][:, 0:rln], in_=src
                                         ).then_inc(S["slab"], 16)

                @blk.gpsimd
                def _(g):
                    joinA(g)
                    for ps in range(NPS):
                        p, sg = ps // 2, ps % 2
                        s0, sl, rl0, rln, rg0 = B_SEGS[sg]
                        if ps >= 1:
                            g.wait_ge(S["sda"], 32 * ps)
                            g.wait_ge(S["vscan"], 32 * ps)
                        for m in range(2):
                            db = 2 * p + m
                            src = bass.AP(dtsp_d.tensor, db * 128 * T + s0,
                                          [[T, 128], [1, sl]])
                            g.dma_start(out=dtsp_s[:, m, 0:sl], in_=src
                                        ).then_inc(S["gslab"], 16)
                        if ps >= 1:
                            g.wait_ge(S["gate"], 2 * ps)
                        for m in range(2):
                            db = 2 * p + m
                            src = bass.AP(siluz_d.tensor, db * 128 * T + s0 + rl0,
                                          [[T, 128], [1, rln]])
                            g.dma_start(out=slz_s[:, m, 0:rln], in_=src
                                        ).then_inc(S["gslab"], 16)
                        for m in range(2):
                            db = 2 * p + m
                            g.wait_ge(S["gate"], 2 * ps + m + 1)
                            dst = bass.AP(yg_d.tensor, db * 128 * TR + rg0,
                                          [[TR, 128], [1, rln]])
                            g.dma_start(out=dst, in_=yg_t[m][:, 0:rln]
                                        ).then_inc(S["oyg"], 16)

                @blk.scalar
                def _(s):
                    s.wait_ge(S["oxc"], 16 * 16 * NCH)
                    for m0 in range(2):
                        src = bass.AP(xc_d.tensor, m0 * 128 * T + B_SEGS[0][0],
                                      [[T, 128], [1, B_SEGS[0][1]]])
                        s.dma_start(out=xc_s[0][:, m0, 0:B_SEGS[0][1]], in_=src
                                    ).then_inc(S["xcl"], 16)
                    for ps in range(NPS):
                        p, sg = ps // 2, ps % 2
                        s0, sl, rl0, rln, rg0 = B_SEGS[sg]
                        s.wait_ge(S["gslab"], 16 * (4 * ps + 2))
                        for n in range(NST):
                            for m in range(2):
                                db = 2 * p + m
                                q = (ps * 16 + n) * 2 + m
                                col = db * 16 + n
                                s.activation(out=r_t[:, 0:sl],
                                             in_=dtsp_s[:, m, 0:sl],
                                             func=AF.Relu, scale=C_M1,
                                             bias=thr[:, col:col + 1])
                                if q >= 2:
                                    s.wait_ge(S["vscan"], q - 1)
                                s.activation(out=dA_t[q % 2][:, 0:sl], in_=r_t[:, 0:sl],
                                             func=AF.Exp, scale=absA[:, col:col + 1],
                                             bias=C_N10).then_inc(S["sda"], 1)
                        if ps + 1 < NPS:
                            pn, sgn = (ps + 1) // 2, (ps + 1) % 2
                            s0n, sln = B_SEGS[sgn][0], B_SEGS[sgn][1]
                            if ps >= 1:
                                s.wait_ge(S["gate"], 2 * ps)
                            for m in range(2):
                                db = 2 * pn + m
                                src = bass.AP(xc_d.tensor, db * 128 * T + s0n,
                                              [[T, 128], [1, sln]])
                                s.dma_start(out=xc_s[(ps + 1) % 2][:, m, 0:sln], in_=src
                                            ).then_inc(S["xcl"], 16)

                @blk.vector
                def _(v):
                    for ps in range(NPS):
                        p, sg = ps // 2, ps % 2
                        s0, sl, rl0, rln, rg0 = B_SEGS[sg]
                        v.wait_ge(S["xcl"], 32 * (ps + 1))
                        v.wait_ge(S["gslab"], 16 * (4 * ps + 2))
                        for m in range(2):
                            v.tensor_mul(u_t[:, m, 0:sl], dtsp_s[:, m, 0:sl],
                                         xc_s[ps % 2][:, m, 0:sl])
                        for n in range(NST):
                            q2 = ps * 16 + n
                            v.wait_ge(S["slab"], slab_base(ps) + 32 * (n + 1))
                            if sg == 0:
                                v.tensor_mul(Bb[q2 % 3][:, 0:HALO],
                                             Bb[q2 % 3][:, 0:HALO], bmask_bc[:])
                            for m in range(2):
                                q = q2 * 2 + m
                                v.tensor_mul(dBu_t[:, 0:sl], u_t[:, m, 0:sl],
                                             Bb[q2 % 3][:, 0:sl])
                                v.wait_ge(S["sda"], q + 1)
                                if sg == 1:
                                    v.wait_ge(S["sstate"], 32 * p + 2 * n + m + 1)
                                init = (0.0 if sg == 0
                                        else state_sb[:, m * 16 + n:m * 16 + n + 1])
                                v.tensor_tensor_scan(
                                    h_t[:, 0:sl], dA_t[q % 2][:, 0:sl], dBu_t[:, 0:sl],
                                    init, ALU.mult, ALU.add).then_inc(S["vscan"], 1)
                                if sg == 0:
                                    v.wait_ge(S["vscan"], q + 1)
                                    v.tensor_copy(state_sb[:, m * 16 + n:m * 16 + n + 1],
                                                  h_t[:, sl - 1:sl]).then_inc(S["sstate"], 1)
                                if q >= 2:
                                    v.wait_ge(S["mmb"], q - 1)
                                v.tensor_mul(hc_t[q % 2][:, 0:rln],
                                             h_t[:, rl0:rl0 + rln],
                                             Cb[q2 % 2][:, 0:rln]).then_inc(S["vhc"], 1)
                        v.wait_ge(S["gslab"], 16 * (4 * ps + 4))
                        for m in range(2):
                            db = 2 * p + m
                            v.wait_ge(S["mmb"], ps * 32 + 31 + m)
                            if ps >= 1:
                                v.wait_ge(S["oyg"], 16 * (2 * (ps - 1) + m + 1))
                            v.scalar_tensor_tensor(
                                t1_t[:, 0:rln], xc_s[ps % 2][:, m, rl0:rl0 + rln],
                                dvec_sb[:, db:db + 1], y_ps[m][:, 0:rln],
                                ALU.mult, ALU.add)
                            v.tensor_mul(yg_t[m][:, 0:rln], t1_t[:, 0:rln],
                                         slz_s[:, m, 0:rln]).then_inc(S["gate"], 1)

                @blk.tensor
                def _(t):
                    for ps in range(NPS):
                        for n in range(NST):
                            for m in range(2):
                                q = (ps * 16 + n) * 2 + m
                                if n == 0 and ps >= 1:
                                    t.wait_ge(S["gate"], 2 * (ps - 1) + m + 1)
                                t.wait_ge(S["vhc"], q + 1)
                                for qq in range(4):
                                    inst = t.matmul(
                                        y_ps[m][:, qq * 512:(qq + 1) * 512],
                                        ident_sb[:],
                                        hc_t[q % 2][:, qq * 512:(qq + 1) * 512],
                                        start=(n == 0), stop=(n == 15))
                                inst.then_inc(S["mmb"], 1)

        def joinB(eng):
            eng.wait_ge(S["oyg"], 16 * 2 * NPS_GLOBAL)

        if upto == "B":
            with nc.Block() as blkf:
                @blkf.scalar
                def _(s):
                    joinB(s)
            return nc

        with ExitStack() as pc:
            EC = pc.enter_context
            w_outT_bf = EC(nc.sbuf_tensor("w_outT_bf", [128, 16, DM], BF16))
            w_f1_bf = EC(nc.sbuf_tensor("w_f1_bf", [128, 8, DM], BF16))
            w_f2_bf = EC(nc.sbuf_tensor("w_f2_bf", [128, 8, DM], BF16))
            w_f3_bf = EC(nc.sbuf_tensor("w_f3_bf", [128, 8, DM], BF16))
            yg_sb = EC(nc.sbuf_tensor("yg_sb", [128, 16, LC], F32))
            yn_bf = EC(nc.sbuf_tensor("yn_bf", [128, 16, LC], BF16))
            xres = EC(nc.sbuf_tensor("xres", [128, 8, LC], F32))
            r1_sb = EC(nc.sbuf_tensor("r1_sb", [128, 8, LC], F32))
            ln1_sb = EC(nc.sbuf_tensor("ln1_sb", [128, 8, LC], F32))
            ln1_bf = EC(nc.sbuf_tensor("ln1_bf", [128, 8, LC], BF16))
            h1_bf = yn_bf
            h2_bf = ln1_bf
            out_sb = ln1_sb
            sq_t = [EC(nc.sbuf_tensor(f"sq_t{i}", [128, LC], F32)) for i in range(2)]
            sqr_t = [EC(nc.sbuf_tensor(f"sqr_t{i}", [128, LC], F32)) for i in range(2)]
            th_t = EC(nc.sbuf_tensor("th_t", [128, LC], F32))
            stat_sb = EC(nc.sbuf_tensor("stat_sb", [1, 4 * LC], F32))
            att_ps = EC(nc.psum_tensor("att_ps", [128, LC], F32))
            ffn_ps = EC(nc.psum_tensor("ffn_ps", [128, LC], F32))
            m_ps = [EC(nc.psum_tensor(f"m_ps{i}", [1, LC], F32)) for i in range(2)]
            ms_ps = [EC(nc.psum_tensor(f"ms_ps{i}", [1, LC], F32)) for i in range(2)]
            bc_ps = [EC(nc.psum_tensor(f"bc_ps{i}", [128, LC], F32)) for i in range(2)]

            RSTD0, MEAN0, VAR0, MR0 = 0, LC, 2 * LC, 3 * LC
            rstd_ap = stat_sb[:, RSTD0:RSTD0 + LC]
            mean_ap = stat_sb[:, MEAN0:MEAN0 + LC]
            var_ap = stat_sb[:, VAR0:VAR0 + LC]
            mr_ap = stat_sb[:, MR0:MR0 + LC]

            def yg_src(cc):
                return bass.AP(yg_d.tensor, cc * LC,
                               [[TR, 128], [128 * TR, 16], [1, LC]])

            def xres_src(cc):
                return bass.AP(xT_in.tensor, HALO + cc * LC,
                               [[T, 128], [128 * T, 8], [1, LC]])

            with nc.Block() as blk:
                @blk.gpsimd
                def _(g):
                    g.wait_ge(S["gate"], 32)
                    g.wait_ge(S["slab"], 16 * 32 * NPS_GLOBAL)
                    g.wait_ge(S["xcl"], 32 * NPS_GLOBAL)
                    g.wait_ge(S["gslab"], 16 * 4 * NPS_GLOBAL)
                    src = bass.AP(w_outT.tensor, 0, [[DM, 128], [128 * DM, 16], [1, DM]])
                    g.dma_start(out=w_outT_bf[:], in_=src).then_inc(S["gwc"], 16)
                    for w_, t_ in [(w_f1, w_f1_bf), (w_f2, w_f2_bf), (w_f3, w_f3_bf)]:
                        src = bass.AP(w_.tensor, 0, [[DM, 128], [128 * DM, 8], [1, DM]])
                        g.dma_start(out=t_[:], in_=src).then_inc(S["gwc"], 16)

                @blk.sync
                def _(sy):
                    joinB(sy)
                    for cc in range(NCC):
                        for o in range(8):
                            sy.wait_ge(S["vout"], cc * 8 + o + 1)
                            dst = bass.AP(out_d.tensor, o * 128 * TR + cc * LC,
                                          [[TR, 128], [1, LC]])
                            sy.dma_start(out=dst, in_=out_sb[:, o, :]
                                         ).then_inc(S["oout"], 16)

                @blk.scalar
                def _(s):
                    joinB(s)
                    s.dma_start(out=yg_sb[:], in_=yg_src(0)).then_inc(S["iyg"], 16)
                    s.dma_start(out=xres[:], in_=xres_src(0)).then_inc(S["ixr"], 16)
                    s.wait_ge(S["gwc"], 64)
                    s.wait_ge(S["iyg"], 16)
                    for k in range(16):
                        if k >= 2:
                            s.wait_ge(S["msqr"], k - 1)
                        s.activation(out=sqr_t[k % 2][:], in_=yg_sb[:, k, :],
                                     func=AF.Square).then_inc(S["sqr"], 1)
                    s.wait_ge(S["msqr"], 16)
                    s.activation(out=rstd_ap, in_=m_ps[0][:],
                                 func=AF.Ln, scale=C1_IDI, bias=C1_RMSE)
                    s.activation(out=rstd_ap, in_=rstd_ap,
                                 func=AF.Exp, scale=C1_MH).then_inc(S["srstd_r"], 1)
                    for cc in range(NCC):
                        par = cc % 2
                        for o in range(8):
                            s.wait_ge(S["vr1"], cc * 8 + o + 1)
                            if cc * 8 + o >= 2:
                                s.wait_ge(S["msql1"], cc * 8 + o - 1)
                            if cc >= 1 and o < 2:
                                s.wait_ge(S["msql2"], 8 * cc)
                            s.activation(out=sq_t[o % 2][:], in_=r1_sb[:, o, :],
                                         func=AF.Square).then_inc(S["sql1"], 1)
                        s.wait_ge(S["vstat"], 4 * cc + 1)
                        s.activation(out=var_ap, in_=var_ap, func=AF.Ln, bias=C1_LNE)
                        s.activation(out=var_ap, in_=var_ap,
                                     func=AF.Exp, scale=C1_MH).then_inc(S["srstd_1"], 1)
                        if cc + 1 < NCC:
                            s.wait_ge(S["vyn"], 16 * (cc + 1))
                            s.wait_ge(S["sqr"], 16 * (cc + 1))
                            s.dma_start(out=yg_sb[:], in_=yg_src(cc + 1)
                                        ).then_inc(S["iyg"], 16)
                            s.wait_ge(S["iyg"], 16 * (cc + 2))
                            for k in range(16):
                                gi = (cc + 1) * 16 + k
                                s.wait_ge(S["msqr"], gi - 1)
                                s.activation(out=sqr_t[gi % 2][:], in_=yg_sb[:, k, :],
                                             func=AF.Square).then_inc(S["sqr"], 1)
                            s.wait_ge(S["msqr"], 16 * (cc + 2))
                            s.wait_ge(S["mbcr"], cc + 1)
                            s.activation(out=rstd_ap, in_=m_ps[par ^ 1][:],
                                         func=AF.Ln, scale=C1_IDI, bias=C1_RMSE)
                            s.activation(out=rstd_ap, in_=rstd_ap,
                                         func=AF.Exp, scale=C1_MH).then_inc(S["srstd_r"], 1)
                        if cc + 1 < NCC:
                            s.wait_ge(S["vr1"], 8 * (cc + 1))
                            s.dma_start(out=xres[:], in_=xres_src(cc + 1)
                                        ).then_inc(S["ixr"], 16)
                        for o in range(8):
                            s.wait_ge(S["mf"], cc * 24 + o + 1)
                            s.copy(h1_bf[:, o, :], (ffn_ps if o % 2 == 0 else att_ps)[:]
                                   ).then_inc(S["sf"], 1)
                        for o in range(8):
                            s.wait_ge(S["mf"], cc * 24 + 8 + o + 1)
                            s.copy(h2_bf[:, o, :], (ffn_ps if o % 2 == 0 else att_ps)[:]
                                   ).then_inc(S["sf"], 1)
                        for o in range(8):
                            s.wait_ge(S["mf"], cc * 24 + 16 + o + 1)
                            if cc * 8 + o >= 1:
                                s.wait_ge(S["vr2"], cc * 8 + o)
                            s.activation(out=th_t[:],
                                         in_=(ffn_ps if o % 2 == 0 else att_ps)[:],
                                         func=AF.Tanh, bias=fbias_sb[:, o:o + 1]
                                         ).then_inc(S["sf"], 1)
                        for o in range(8):
                            s.wait_ge(S["vr2"], cc * 8 + o + 1)
                            if cc * 8 + o >= 2:
                                s.wait_ge(S["msql2"], cc * 8 + o - 1)
                            if o < 2:
                                s.wait_ge(S["msql1"], 8 * (cc + 1))
                            s.activation(out=sq_t[o % 2][:], in_=r1_sb[:, o, :],
                                         func=AF.Square).then_inc(S["sql2"], 1)
                        s.wait_ge(S["vstat"], 4 * cc + 3)
                        s.activation(out=var_ap, in_=var_ap, func=AF.Ln, bias=C1_LNE)
                        s.activation(out=var_ap, in_=var_ap,
                                     func=AF.Exp, scale=C1_MH).then_inc(S["srstd_2"], 1)


                @blk.tensor
                def _(t):
                    t.wait_ge(S["gwc"], 64)
                    for k in range(16):
                        t.wait_ge(S["sqr"], k + 1)
                        t.matmul(m_ps[0][:], ones_col[:], sqr_t[k % 2][:],
                                 start=(k == 0), stop=(k == 15)).then_inc(S["msqr"], 1)
                    t.wait_ge(S["srstd_r"], 1)
                    t.matmul(att_ps[:], ones_row[:], rstd_ap,
                             start=True, stop=True).then_inc(S["mbcr"], 1)
                    for cc in range(NCC):
                        par = cc % 2
                        t.wait_ge(S["vyn"], 16 * (cc + 1))
                        for o in range(8):
                            if cc * 8 + o >= 1:
                                t.wait_ge(S["vr1"], cc * 8 + o)
                            for k in range(16):
                                inst = t.matmul(att_ps[:],
                                                w_outT_bf[:, k, o * 128:(o + 1) * 128],
                                                yn_bf[:, k, :],
                                                start=(k == 0), stop=(k == 15))
                            inst.then_inc(S["matt"], 1)
                        t.wait_ge(S["srstd_r"], cc + 1)
                        if cc >= 1:
                            t.wait_ge(S["vstat"], 4 * (cc - 1) + 3)
                        for o in range(8):
                            t.wait_ge(S["vr1"], cc * 8 + o + 1)
                            inst = t.matmul(m_ps[par][:], ones_col[:], r1_sb[:, o, :],
                                            start=(o == 0), stop=(o == 7))
                        inst.then_inc(S["mstat1"], 1)
                        for o in range(8):
                            t.wait_ge(S["sql1"], cc * 8 + o + 1)
                            t.matmul(ms_ps[par][:], ones_col[:], sq_t[o % 2][:],
                                     start=(o == 0), stop=(o == 7)).then_inc(S["msql1"], 1)
                        t.wait_ge(S["srstd_1"], cc + 1)
                        t.wait_ge(S["vstat"], 4 * cc + 2)
                        if cc >= 1:
                            t.wait_ge(S["vout"], 8 * cc)
                        t.matmul(bc_ps[0][:], ones_row[:], var_ap, start=True, stop=True)
                        t.matmul(bc_ps[1][:], ones_row[:], mr_ap,
                                 start=True, stop=True).then_inc(S["mbc1"], 1)
                        if cc + 1 < NCC:
                            if cc >= 1:
                                t.wait_ge(S["vstat"], 4 * (cc - 1) + 3)
                            for k in range(16):
                                gi = (cc + 1) * 16 + k
                                t.wait_ge(S["sqr"], gi + 1)
                                t.matmul(m_ps[par ^ 1][:], ones_col[:], sqr_t[gi % 2][:],
                                         start=(k == 0), stop=(k == 15)
                                         ).then_inc(S["msqr"], 1)
                        t.wait_ge(S["vln"], 8 * (cc + 1))
                        for ff in range(3):
                            wt = [w_f1_bf, w_f2_bf, w_f3_bf][ff]
                            src_bf = [ln1_bf, h1_bf, h2_bf][ff]
                            if ff > 0:
                                t.wait_ge(S["sf"], cc * 24 + 8 * ff)
                            for o in range(8):
                                gf = cc * 24 + 8 * ff + o
                                if ff == 0 and o == 1:
                                    t.wait_ge(S["vr1"], 8 * (cc + 1))
                                if gf >= 2:
                                    t.wait_ge(S["sf"], gf - 1)
                                bank = ffn_ps if o % 2 == 0 else att_ps
                                for k in range(8):
                                    inst = t.matmul(bank[:],
                                                    wt[:, k, o * 128:(o + 1) * 128],
                                                    src_bf[:, k, :],
                                                    start=(k == 0), stop=(k == 7))
                                inst.then_inc(S["mf"], 1)
                        if cc + 1 < NCC:
                            t.wait_ge(S["srstd_r"], cc + 2)
                            t.wait_ge(S["sf"], 24 * (cc + 1))
                            t.matmul(att_ps[:], ones_row[:], rstd_ap,
                                     start=True, stop=True).then_inc(S["mbcr"], 1)
                        t.wait_ge(S["vstat"], 4 * cc + 1)
                        for o in range(8):
                            t.wait_ge(S["vr2"], cc * 8 + o + 1)
                            inst = t.matmul(m_ps[par][:], ones_col[:], r1_sb[:, o, :],
                                            start=(o == 0), stop=(o == 7))
                        inst.then_inc(S["mstat2"], 1)
                        for o in range(8):
                            t.wait_ge(S["sql2"], cc * 8 + o + 1)
                            t.matmul(ms_ps[par][:], ones_col[:], sq_t[o % 2][:],
                                     start=(o == 0), stop=(o == 7)).then_inc(S["msql2"], 1)
                        t.wait_ge(S["srstd_2"], cc + 1)
                        t.wait_ge(S["vstat"], 4 * cc + 4)
                        t.wait_ge(S["vln"], 8 * (cc + 1))
                        t.matmul(bc_ps[0][:], ones_row[:], var_ap, start=True, stop=True)
                        t.matmul(bc_ps[1][:], ones_row[:], mr_ap,
                                 start=True, stop=True).then_inc(S["mbc2"], 1)

                @blk.vector
                def _(v):
                    def emit_yn(cc):
                        v.wait_ge(S["mbcr"], cc + 1)
                        v.wait_ge(S["iyg"], 16 * (cc + 1))
                        for k in range(16):
                            v.scalar_tensor_tensor(
                                yn_bf[:, k, :], yg_sb[:, k, :], rmsw_sb[:, k:k + 1],
                                att_ps[:], ALU.mult, ALU.mult).then_inc(S["vyn"], 1)

                    emit_yn(0)
                    for cc in range(NCC):
                        par = cc % 2
                        v.wait_ge(S["ixr"], 16 * (cc + 1))
                        for o in range(8):
                            v.wait_ge(S["matt"], cc * 8 + o + 1)
                            v.tensor_add(r1_sb[:, o, :], att_ps[:],
                                         xres[:, o, :]).then_inc(S["vr1"], 1)
                        v.wait_ge(S["mstat1"], cc + 1)
                        v.tensor_scalar_mul(mean_ap, m_ps[par][:], 1.0 / DM)
                        v.tensor_mul(mr_ap, mean_ap, mean_ap)
                        v.wait_ge(S["msql1"], 8 * (cc + 1))
                        v.scalar_tensor_tensor(var_ap, ms_ps[par][:], 1.0 / DM, mr_ap,
                                               ALU.mult, ALU.subtract
                                               ).then_inc(S["vstat"], 1)
                        v.wait_ge(S["srstd_1"], cc + 1)
                        v.tensor_mul(mr_ap, mean_ap, var_ap).then_inc(S["vstat"], 1)
                        v.wait_ge(S["mbc1"], cc + 1)
                        if cc >= 1:
                            v.wait_ge(S["oout"], 16 * 8 * cc)
                        for o in range(8):
                            v.tensor_mul(ln1_sb[:, o, :], r1_sb[:, o, :], bc_ps[0][:])
                            v.tensor_sub(ln1_sb[:, o, :], ln1_sb[:, o, :], bc_ps[1][:])
                            v.tensor_copy(ln1_bf[:, o, :], ln1_sb[:, o, :]
                                          ).then_inc(S["vln"], 1)
                        v.wait_ge(S["mstat1"], cc + 1)
                        for o in range(8):
                            v.wait_ge(S["sf"], cc * 24 + 16 + o + 1)
                            v.wait_ge(S["sql1"], cc * 8 + o + 1)
                            v.tensor_add(r1_sb[:, o, :], ln1_sb[:, o, :],
                                         th_t[:]).then_inc(S["vr2"], 1)
                        v.wait_ge(S["mstat2"], cc + 1)
                        v.tensor_scalar_mul(mean_ap, m_ps[par][:], 1.0 / DM)
                        v.tensor_mul(mr_ap, mean_ap, mean_ap)
                        v.wait_ge(S["msql2"], 8 * (cc + 1))
                        v.scalar_tensor_tensor(var_ap, ms_ps[par][:], 1.0 / DM, mr_ap,
                                               ALU.mult, ALU.subtract
                                               ).then_inc(S["vstat"], 1)
                        v.wait_ge(S["srstd_2"], cc + 1)
                        v.tensor_mul(mr_ap, mean_ap, var_ap).then_inc(S["vstat"], 1)
                        if cc + 1 < NCC:
                            emit_yn(cc + 1)
                        v.wait_ge(S["mbc2"], cc + 1)
                        for o in range(8):
                            v.tensor_mul(out_sb[:, o, :], r1_sb[:, o, :], bc_ps[0][:])
                            v.tensor_sub(out_sb[:, o, :], out_sb[:, o, :], bc_ps[1][:]
                                         ).then_inc(S["vout"], 1)

        with nc.Block() as blkf:
            @blkf.scalar
            def _(s):
                s.wait_ge(S["oout"], 16 * 8 * NCC)
        return nc


def _layout_inputs(inputs):
    import ml_dtypes
    x = np.asarray(inputs["L0_attn_x"], np.float32)

    def c(a):
        return np.ascontiguousarray(a)

    shared = dict(
        w_inT=c(np.asarray(inputs["L0_attn_W_in"], np.float32).T),
        w_xT=c(np.asarray(inputs["L0_attn_W_x"], np.float32).T),
        w_dtT=c(np.asarray(inputs["L0_attn_W_dt"], np.float32).T),
        w_outT=c(np.asarray(inputs["L0_attn_W_out"], np.float32).T),
        w_f1=c(np.asarray(inputs["L0_ffn_weight"], np.float32)),
        w_f2=c(np.asarray(inputs["L0_ffn_aux_weight1"], np.float32)),
        w_f3=c(np.asarray(inputs["L0_ffn_aux_weight2"], np.float32)),
        a_re=c(np.asarray(inputs["L0_attn_A_log"], np.float32)
               .reshape(16, 128, NST).transpose(1, 0, 2).reshape(128, 256)),
        wc_re=c(np.asarray(inputs["L0_attn_W_conv"], np.float32)
                .reshape(16, 128, 4).transpose(1, 0, 2).reshape(128, 64)),
        bconv_re=c(np.asarray(inputs["L0_attn_b_conv"], np.float32).reshape(16, 128).T),
        bdt_re=c(np.asarray(inputs["L0_attn_b_dt"], np.float32).reshape(16, 128).T),
        d_re=c(np.asarray(inputs["L0_attn_D"], np.float32).reshape(16, 128).T),
        rmsw_re=c(np.asarray(inputs["L0_attn_rms_w"], np.float32).reshape(16, 128).T),
        fbias_re=c(np.asarray(inputs["L0_ffn_bias"], np.float32).reshape(8, 128).T),
        ident_in=np.eye(128, dtype=ml_dtypes.bfloat16),
        consts_in=np.tile(np.array([[E_EPS, -10.0, -1.0, 1.0 / DI, RMSEPS,
                                     -0.5, LNEPS, 0.0]], np.float32), (128, 1)),
    )
    in_maps = []
    for core in range(NCORES):
        b, hf = core // 2, core % 2
        xT = np.zeros((DM, T), np.float32)
        s0 = hf * TR - HALO
        if s0 < 0:
            xT[:, HALO:] = x[b, 0:TR].T
            bmask = np.zeros((1, HALO), np.float32)
        else:
            xT[:, :] = x[b, s0:hf * TR + TR].T
            bmask = np.ones((1, HALO), np.float32)
        m = dict(shared)
        m["xT_in"] = xT
        m["bmask_in"] = bmask
        in_maps.append(m)
    return in_maps


_CACHE = {}


def kernel(**inputs):
    in_maps = _layout_inputs(inputs)
    if "nc" not in _CACHE:
        _CACHE["nc"] = build_nc("C")
    nc = _CACHE["nc"]
    res = run_bass_kernel_spmd(nc, in_maps, core_ids=list(range(NCORES)))
    out = np.empty((4, 8192, DM), np.float32)
    for core in range(NCORES):
        b, hf = core // 2, core % 2
        out[b, hf * TR:(hf + 1) * TR, :] = res.results[core]["out_d"].T
    return out
